# version 03
# baseline (speedup 1.0000x reference)
import contextlib
import numpy as np
import ml_dtypes
import concourse.bass as bass
import concourse.mybir as mybir
from concourse.bass_utils import run_bass_kernel_spmd

F32 = mybir.dt.float32
BF16 = mybir.dt.bfloat16
I32 = mybir.dt.int32
AF = mybir.ActivationFunctionType
ALU = mybir.AluOpType
AX = mybir.AxisListType
BF = ml_dtypes.bfloat16

NTOK = 4096
NBLK = NTOK // 128
TTILE = 1024
NTILES = NTOK // TTILE
NSTEP = TTILE // 16
GELU_A = 0.79788456
GELU_C3 = 0.1070322243
MAGIC = 0x5f3759df
STAGE = 4
NTILES_RUN = 4
NSTEPS_DBG = 99
STEP_CUT = 99
GCUT = 99
LCUT = 99
TCUT = 99
MM_LIMIT = 99


class P:
    ENGS = ("tensor", "vector", "scalar", "gpsimd", "sync")
    CHS = ("cst", "s0", "s1", "s2", "w10", "w11", "w2r", "pj0", "pj1", "sto", "st0", "st1", "xo")

    def __init__(self):
        self.q = {e: [] for e in self.ENGS}
        self.cnt = {e: 0 for e in list(self.ENGS) + list(self.CHS)}
        allsems = list(self.ENGS) + list(self.CHS)
        self.wm = {c: {pr: 0 for pr in allsems} for c in self.ENGS}

    def emit(self, eng, fn, waits=(), inc=1, sem=None, drain=False):
        sem = sem or eng
        rw = []
        for (we, wv) in waits:
            if we == eng and eng != "sync":
                continue
            if wv > self.wm[eng][we]:
                rw.append((we, wv))
                self.wm[eng][we] = wv
        self.q[eng].append((tuple(rw), fn, inc, sem, drain))
        if inc:
            self.cnt[sem] += inc
        return (sem, self.cnt[sem])

    def token(self, sem):
        return (sem, self.cnt[sem])

    def wave_end(self, eng="tensor"):
        rw, fn, inc, sem, drain = self.q[eng][-1]
        assert inc is None
        self.q[eng][-1] = (rw, fn, 1, sem, drain)
        self.cnt[sem] += 1
        return (sem, self.cnt[sem])


def build_program():
    nc = bass.Bass("TRN2", target_bir_lowering=False, debug=False)
    p = P()

    def din(name, shape, dt=F32):
        return nc.dram_tensor(name, list(shape), dt, kind="ExternalInput").ap()

    x_in = din("x", [NTOK, 1024])
    wqT = din("wqT", [1024, 1024]); wkT = din("wkT", [1024, 1024])
    wvT = din("wvT", [1024, 1024]); wgT = din("wgT", [1024, 1024])
    woT = din("woT", [1024, 1024])
    q1 = din("q1", [1024, 4096], I32); sc1 = din("sc1", [8, 4096])
    q2 = din("q2", [4096, 1024], I32); sc2 = din("sc2", [32, 1024])
    fc1b = din("fc1b", [128, 32])
    fc2b = din("fc2b", [1, 1024])
    lng = din("lng", [1, 1024]); lnbta = din("lnbta", [1, 1024])
    postw = din("postw", [1, 1024]); postb = din("postb", [1, 1024])
    lrwT = din("lrwT", [1024, 16]); lrb = din("lrb", [16, 1])
    ltk = din("ltk", [1, 16]); recip16 = din("recip16", [1, 16])
    lnw_slot = din("lnw_slot", [128, 256]); lnb_slot = din("lnb_slot", [128, 256])
    lnwT8 = din("lnwT8", [128, 8])
    w1s0 = din("w1s0", [128, 2048]); w2s0 = din("w2s0", [128, 2048])
    w2st0 = din("w2st0", [128, 2048])
    b1s0 = din("b1s0", [128, 2048]); b2s0 = din("b2s0", [128, 256])
    identc = din("identc", [128, 128], BF16)
    ntrilT = din("ntrilT", [128, 16])
    eye16p = din("eye16p", [16, 32], BF16)
    onesr = din("onesr", [1, 128], BF16)
    onesP32 = din("onesP32", [128, 32], BF16)

    out = nc.dram_tensor("out", [NTOK, 1024], F32, kind="ExternalOutput").ap()

    x1stash = nc.dram_tensor("x1stash", [NTOK, 1024], BF16, kind="Internal").ap()
    x2stash = nc.dram_tensor("x2stash", [NTOK, 1024], BF16, kind="Internal").ap()

    def scr(name, shape, dt=BF16):
        return nc.dram_tensor(name, list(shape), dt, kind="Internal").ap()

    w1stash = scr("w1stash", [32, 8, 128, 128])
    w2stash = scr("w2stash", [32, 128, 1024])
    projstash = scr("projstash", [5, 8, 128, 1024])
    lrwstash = scr("lrwstash", [8, 128, 16])

    ctx = contextlib.ExitStack()
    sb = lambda n, s, d=BF16: ctx.enter_context(nc.sbuf_tensor(n, s, d))
    psal = lambda n, s, d=F32: ctx.enter_context(nc.psum_tensor(n, s, d))

    PSPOOL = psal("PSPOOL", [128, 4096])
    POOL16 = PSPOOL[:, :].bitcast(BF16)
    PS_A = PSPOOL[:, 0:1024]
    PS_B = PSPOOL[:, 1024:2048]
    PS_C = PSPOOL[:, 2048:2560]
    PS_F = PSPOOL[:, 2560:3072]
    PS_D = POOL16[:, 7168:8192]
    PS_E = POOL16[:, 6144:7168]
    def Z1PS(g):
        return PSPOOL[:, 512 * g:512 * g + 256]
    def GZ1PS(g):
        return (PSPOOL[:, 2048:2304] if g == 0 else
                PSPOOL[:, 2560:2816] if g == 1 else
                PSPOOL[:, 2304:2560] if g == 2 else
                PSPOOL[:, 3072:3328])
    Z2PS = PSPOOL[:, 2048:2304]
    def A1PS(g):
        return (PSPOOL[:, 3968 + (g // 2) * 16: 3984 + (g // 2) * 16] if g % 2 == 0
                else PSPOOL[:, 2816 + (g // 2) * 16: 2832 + (g // 2) * 16])
    def A2PS(g):
        return (PSPOOL[:, 4000 + (g // 2) * 16: 4016 + (g // 2) * 16] if g % 2 == 0
                else PSPOOL[:, 2848 + (g // 2) * 16: 2864 + (g // 2) * 16])

    BIG = sb("BIG", [128, 32768])
    W2AR = sb("W2AR", [128, 32768])
    HT = sb("HT", [128, 8192])
    XIN = sb("XIN", [128, 8192])
    GATE = sb("GATE", [128, 8192])
    XF0 = sb("XF0", [128, 1024], F32)
    XF1 = sb("XF1", [128, 1024], F32)
    SQS = sb("SQS", [128, 1024])
    TMPA = sb("TMPA", [128, 1024])
    TMPB = sb("TMPB", [128, 1024])
    STAT = sb("STAT", [128, 64], F32)
    IC = sb("IC", [128, 8], I32)
    FC1B = sb("FC1B", [128, 32], F32)
    CB = sb("CB", [128, 5632])
    IDE = sb("IDE", [128, 128])
    LNWS = sb("LNWS", [128, 256]); LNBS = sb("LNBS", [128, 256]); AS_ = sb("AS_", [128, 256])
    LNWT8 = sb("LNWT8", [128, 8], F32)
    LNWB = sb("LNWB", [128, 256])
    ONESF = sb("ONESF", [128, 32], F32)
    NTT = sb("NTT", [128, 16], F32)
    NTTE = sb("NTTE", [128, 16])
    EYE = sb("EYE", [16, 32])
    ONESR = sb("ONESR", [1, 128])
    ONESP = sb("ONESP", [128, 32])
    TE = sb("TE", [1, 16], F32)
    TEB = sb("TEB", [1, 16])
    T15 = sb("T15", [128, 1], F32)
    LRB2 = sb("LRB2", [16, 1], F32)
    FEV = GATE[:, 0:2048].bitcast(F32)
    XB16 = GATE[:, 2048:3072]
    W1B0 = GATE[:, 3072:4096]; W1B1 = GATE[:, 4096:5120]
    XB17 = GATE[:, 5120:6144]
    carve_pos = [0]
    def carve(width_bf16, dt=BF16):
        a = carve_pos[0]
        carve_pos[0] += width_bf16
        v = W2AR[:, a:a + width_bf16]
        return v.bitcast(dt) if dt != BF16 else v
    W1S = carve(2048); W2S = carve(2048); W2ST = carve(2048)
    B1S = carve(2048); B2S = carve(256)
    GXB = carve(512, F32); XHB = carve(512, F32)
    XBT = carve(1024); UB = carve(1024); TB = carve(1024)
    S1B = carve(1024)
    BWD = carve(1024); X2SB = carve(1024)
    GZ1 = carve(1024); LGZ1 = carve(1024); X2TS = carve(1024)
    GZ2 = carve(256); LGZ2 = carve(256); GZ2T = carve(256)
    XQWS = carve(256); SIGS = carve(512, F32); LASTC = carve(512, F32)
    R1T = carve(128); R2T = carve(128)
    GX2 = carve(512, F32)
    XKTS = carve(256)
    SRAW_full = carve(1024); SBLK_full = carve(1024)
    SRAW = SRAW_full[0:16, :]; SBLK = SBLK_full[0:16, :]
    ZB2 = SRAW_full.bitcast(F32)[:, 0:256]
    LRW = carve(128)
    PJW = carve(8192)
    assert carve_pos[0] <= 32768, carve_pos[0]

    sems = {e: ctx.enter_context(nc.semaphore("sem_" + e))
            for e in list(P.ENGS) + list(P.CHS)}

    def dma(dst, src, waits=(), ch="cst", eng="sync"):
        w = list(waits)
        if p.cnt[ch]:
            w.append((ch, p.cnt[ch]))
        return p.emit(eng, lambda e, d=dst, s=src: e.dma_start(out=d, in_=s),
                      waits=w, inc=16, sem=ch)

    def mm(out_, lhsT, rhs, start, stop, tp, waits=(), inc=1):
        assert lhsT.dtype == rhs.dtype, (lhsT.name, lhsT.dtype, rhs.name, rhs.dtype)
        def f(e, o=out_, l=lhsT, r=rhs, st=start, sp=stop, t=tp):
            return e.matmul(o, l, r, start=st, stop=sp, tile_position=t)
        return p.emit("tensor", f, waits=waits, inc=inc)

    def tr(out_, in_, ident, tp, waits=(), inc=1):
        def f(e, o=out_, i=in_, d=ident, t=tp):
            return e.transpose(o, i, d, tile_position=t)
        return p.emit("tensor", f, waits=waits, inc=inc)

    def vop(fn, waits=(), inc=1, eng="vector"):
        return p.emit(eng, fn, waits=waits, inc=inc, drain=True)

    def act(out_, in_, func, bias=0.0, scale=1.0, accum=None, waits=(), inc=1):
        def f(e, o=out_, i=in_, fu=func, b=bias, s=scale, a=accum):
            if a is None:
                return e.activation(o, i, fu, bias=b, scale=s)
            return e.activation(o, i, fu, bias=b, scale=s, accum_out=a)
        return p.emit("scalar", f, waits=waits, inc=inc, drain=True)

    def rsqrt_ops(out_, v, scrp, w, waits=(), iters=4, eng="vector"):
        tok = vop(lambda e, o=out_, vv=v: e.tensor_scalar(
            o.bitcast(I32), vv.bitcast(I32), 1, None, ALU.logical_shift_right),
            waits=waits, eng=eng)
        tok = vop(lambda e, o=out_: e.tensor_tensor(
            o.bitcast(I32), IC[:, 0:w].bitcast(I32), o.bitcast(I32), ALU.subtract), eng=eng)
        for _ in range(iters):
            tok = vop(lambda e, o=out_, s=scrp: e.tensor_tensor(s, o, o, ALU.mult), eng=eng)
            tok = vop(lambda e, s=scrp, vv=v: e.tensor_tensor(s, s, vv, ALU.mult), eng=eng)
            tok = vop(lambda e, s=scrp: e.tensor_scalar(s, s, -0.5, 1.5, ALU.mult, ALU.add), eng=eng)
            tok = vop(lambda e, o=out_, s=scrp: e.tensor_tensor(o, o, s, ALU.mult), eng=eng)
        return tok

    def ln_stats(src, W, c0, eps, waits=(), eng="vector"):
        negmu = STAT[:, c0:c0 + 1]
        rstd = STAT[:, c0 + 1:c0 + 2]
        s1 = STAT[:, c0 + 2:c0 + 3]
        s2 = STAT[:, c0 + 3:c0 + 4]
        scp = STAT[:, c0 + 4:c0 + 5]
        t_red = vop(lambda e, s=src: e.reduce_sum(s1, s, axis=AX.X), waits=waits, eng=eng)
        t_sq = act(SQS[:, 0:W], src, AF.Square, accum=s2,
                   waits=list(waits) + [p.token("xo")])
        tok = vop(lambda e: e.tensor_scalar(negmu, s1, -1.0 / W, None, ALU.mult),
                  waits=[t_red], eng=eng)
        tok = vop(lambda e: e.tensor_scalar(s1, s2, 1.0 / W, None, ALU.mult),
                  waits=[t_sq], eng=eng)
        tok = vop(lambda e: e.tensor_tensor(scp, negmu, negmu, ALU.mult), eng=eng)
        tok = vop(lambda e: e.tensor_tensor(s1, s1, scp, ALU.subtract), eng=eng)
        tok = vop(lambda e: e.tensor_scalar(s1, s1, 0.0, 1e30, ALU.max, ALU.min), eng=eng)
        tok = vop(lambda e, ep=eps: e.tensor_scalar(s1, s1, ep, None, ALU.add), eng=eng)
        tok = rsqrt_ops(rstd, s1, scp, 1, eng=eng)
        return tok

    def prep():
        vop(lambda e: e.memset(PSPOOL[:, :], 0.0))
        vop(lambda e: e.memset(IC[:, :], MAGIC))
        ld = {}
        ld["ide"] = dma(IDE[:, :], identc)
        ld["fc1b"] = dma(FC1B[:, :], fc1b)
        ld["ntt"] = dma(NTT[:, :], ntrilT)
        ld["eye"] = dma(EYE[:, :], eye16p)
        ld["onesr"] = dma(ONESR[:, :], onesr)
        ld["onesp"] = dma(ONESP[:, :], onesP32)
        ld["r16"] = dma(STAT[0:1, 32:48], recip16)
        ld["lrb"] = dma(STAT[0:16, 48:49], lrb)
        ld["lnwT8"] = dma(LNWT8[:, :], lnwT8)
        ld["te"] = dma(TE[:, :], ltk)
        tok_cst = p.token("cst")
        guard = p.token("vector")
        for i, (srcb, scl) in enumerate([(fc2b, 0.25), (lng, 1.0), (lnbta, 1.0),
                                         (postw, 1.0), (postb, 1.0)]):
            tldb = dma(XF0[:, :], srcb.broadcast_to((128, 1024)), waits=[guard], ch="s0")
            guard = vop(lambda e, ii=i, sc=scl: e.tensor_scalar(
                CB[:, ii * 1024:(ii + 1) * 1024], XF0[:, :], sc, None, ALU.mult),
                waits=[tldb])
        tw1 = dma(XF0[:, 0:256], lnw_slot, waits=[guard], ch="s0")
        tw2 = dma(XF1[:, 0:256], lnb_slot, ch="s1")
        vop(lambda e: e.tensor_copy(LNWS[:, :], XF0[:, 0:256]), waits=[tw1])
        vop(lambda e: e.tensor_copy(LNBS[:, :], XF1[:, 0:256]), waits=[tw2])
        vop(lambda e: e.tensor_tensor(AS_[:, :], XF0[:, 0:256], XF0[:, 0:256], ALU.mult))
        vop(lambda e: e.tensor_tensor(LNWB[:, :], LNWS[:, :], LNBS[:, :], ALU.mult))
        vop(lambda e: e.memset(ONESF[:, :], 1.0))
        vop(lambda e: e.tensor_scalar(LRB2[:, :], STAT[0:16, 48:49], 0.5, None, ALU.mult),
            waits=[tok_cst])
        vop(lambda e: e.tensor_tensor(TE[:, :], TE[:, :], STAT[0:1, 32:48], ALU.add))
        te_tok = vop(lambda e: e.tensor_scalar(TE[:, :], TE[:, :], 0.0, None, ALU.max))
        te_tok = vop(lambda e: e.tensor_copy(TEB[:, :], TE[:, :]))
        m1 = mm(PS_C[:, 0:16], ONESR[0:1, :], TEB[0:1, :], True, True, (0, 0),
                waits=[te_tok])
        m2 = mm(PS_C[:, 16:17], ONESR[0:1, :], TEB[0:1, 15:16], True, True, (0, 0))
        vop(lambda e: e.tensor_tensor(NTTE[:, :], NTT[:, :], PS_C[:, 0:16], ALU.mult),
            waits=[m1, m2])
        vop(lambda e: e.tensor_copy(T15[:, :], PS_C[:, 16:17]), waits=[m2])

        xf_guard = [p.token("vector"), p.token("vector")]
        tmp_guard = [p.token("st0"), p.token("st1")]
        for wi, wsrc in enumerate([wqT, wkT, wvT, wgT, woT]):
            for c in range(8):
                par = c % 2
                buf = XF0 if par == 0 else XF1
                tmp = TMPA if par == 0 else TMPB
                tld = dma(buf[:, :], wsrc[c * 128:(c + 1) * 128, :], waits=[xf_guard[par]],
                          ch="s0" if par == 0 else "s1")
                tcv = vop(lambda e, b=buf, t=tmp: e.tensor_copy(t[:, :], b[:, :]),
                          waits=[tld, tmp_guard[par]])
                tst = dma(projstash[wi, c], tmp[:, :], waits=[tcv],
                          ch="st0" if par == 0 else "st1")
                xf_guard[par] = tcv
                tmp_guard[par] = tst

        qi = BIG[:, 0:8192].bitcast(I32)
        scb = BIG[:, 8192:16384].bitcast(F32)
        qf = BIG[:, 16384:24576].bitcast(F32)
        wbf = BIG[:, 24576:28672]
        ta_prev = p.token("scalar")
        tw_prev = p.token("vector")
        tst_prev = p.token("sync")
        for c in range(8):
            t1 = dma(qi, q1[c * 128:(c + 1) * 128, :], waits=[ta_prev], ch="s0")
            t2 = dma(scb, sc1[c:c + 1, :].broadcast_to((128, 4096)), waits=[tw_prev], ch="s1")
            ta = act(qf, qi, AF.Copy, bias=-128.0, waits=[t1, tw_prev])
            tw = vop(lambda e: e.tensor_tensor(wbf, qf, scb, ALU.mult),
                     waits=[ta, t2, tst_prev])
            tst = dma(w1stash[:, c].rearrange("f k n -> k f n"),
                      wbf.rearrange("k (f n) -> k f n", n=128), waits=[tw], ch="sto")
            ta_prev, tw_prev, tst_prev = ta, tw, tst
        qi2 = BIG[:, 0:2048].bitcast(I32)
        scb2 = BIG[:, 2048:4096].bitcast(F32)
        qf2 = BIG[:, 4096:6144].bitcast(F32)
        wbf2 = BIG[:, 6144:7168]
        for c in range(32):
            t1 = dma(qi2, q2[c * 128:(c + 1) * 128, :], waits=[ta_prev], ch="s0")
            t2 = dma(scb2, sc2[c:c + 1, :].broadcast_to((128, 1024)), waits=[tw_prev], ch="s1")
            ta = act(qf2, qi2, AF.Copy, bias=-128.0, waits=[t1, tw_prev])
            tw = vop(lambda e: e.tensor_tensor(wbf2, qf2, scb2, ALU.mult),
                     waits=[ta, t2, tst_prev])
            tst = dma(w2stash[c], wbf2, waits=[tw], ch="sto")
            ta_prev, tw_prev, tst_prev = ta, tw, tst

    def ffn_phase(first):
        src = x_in if first else x2stash
        w2tok = []
        for f in range(32):
            w2tok.append(dma(W2AR[:, f * 1024:(f + 1) * 1024], w2stash[f],
                             waits=[p.token("sto"), p.token("vector")], ch="w2r",
                             eng="gpsimd"))
        for st in range(4):
            blk_toks = []
            for b in range(8):
                tb = st * 8 + b
                par = b % 2
                c0 = 8 * par
                xf = (XF0 if par == 0 else XF1) if first else (XB16 if par == 0 else XB17)
                tld = dma(xf[:, :], src[tb * 128:(tb + 1) * 128, :],
                          waits=[p.token("vector")] + ([] if first else [p.token("sto")]),
                          ch="s0" if par == 0 else "s1")
                t_st = ln_stats(xf[:, :], 1024, c0, 1e-5, waits=[tld])
                xin_blk = XIN[:, b * 1024:(b + 1) * 1024]
                t_xin = vop(lambda e, x=xf, o=xin_blk, cc=c0: e.tensor_scalar(
                    o, x[:, :], STAT[:, cc:cc + 1], STAT[:, cc + 1:cc + 2],
                    ALU.add, ALU.mult), waits=[t_st])
                t_st2 = ln_stats(xin_blk, 1024, c0, 1e-5, waits=[t_xin])
                t_xh = vop(lambda e, o=TMPA, i=xin_blk, cc=c0: e.tensor_scalar(
                    o[:, :], i, STAT[:, cc:cc + 1], STAT[:, cc + 1:cc + 2],
                    ALU.add, ALU.mult), waits=[t_st2, p.token("st0"), p.token("st1")])
                t_h = vop(lambda e: e.tensor_tensor(TMPB[:, :], TMPA[:, :],
                                                    CB[:, 1024:2048], ALU.mult))
                t_h = vop(lambda e: e.tensor_tensor(TMPB[:, :], TMPB[:, :],
                                                    CB[:, 2048:3072], ALU.add))
                ttr = []
                for k in range(8):
                    ttr.append(tr(PS_E[:, k * 128:(k + 1) * 128], TMPB[:, k * 128:(k + 1) * 128],
                                  IDE[:, :], (0, 0), waits=[t_h] if k == 0 else (),
                                  inc=(1 if k == 7 else None)))
                t_ev = vop(lambda e, t_=tb: e.tensor_copy(
                    HT[:, :].rearrange("q (c n) -> q c n", c=8)[:, :, (t_ % 8) * 128:(t_ % 8 + 1) * 128],
                    PS_E[:, :].rearrange("q (c n) -> q c n", c=8)), waits=[ttr[-1]])
                blk_toks.append(t_ev)
            silu_toks = []
            last_mm_for_buf = [p.token("tensor"), p.token("tensor")]
            last_silu_ps = [p.token("scalar"), p.token("scalar")]
            for f in range(32):
                par = f % 2
                wb = W1B0 if par == 0 else W1B1
                yps = PS_A if par == 0 else PS_B
                tw = dma(wb[:, :].rearrange("k (c n) -> k c n", c=8),
                         w1stash[f].rearrange("c k n -> k c n"),
                         waits=[last_mm_for_buf[par], p.token("sto")],
                         ch="w10" if par == 0 else "w11")
                mt = None
                for half in range(2):
                    for c in range(8):
                        mt = mm(yps[:, half * 512:(half + 1) * 512],
                                wb[:, c * 128:(c + 1) * 128],
                                HT[:, c * 1024 + half * 512: c * 1024 + (half + 1) * 512],
                                c == 0, c == 7, (0, 0),
                                waits=([tw, last_silu_ps[par]] + (blk_toks if f == 0 else []))
                                if (c == 0 and half == 0) else (),
                                inc=(1 if (c == 7 and half == 1) else None))
                last_mm_for_buf[par] = mt
                t_si = act(BIG[:, f * 1024:(f + 1) * 1024], yps[:, :], AF.Silu,
                           bias=FC1B[:, f:f + 1], waits=[mt])
                silu_toks.append(t_si)
                last_silu_ps[par] = t_si
            last_ev = [p.token("vector"), p.token("vector")]
            for tg in range(8):
                tb = st * 8 + tg
                par = tg % 2
                zps = PS_A if par == 0 else PS_B
                mt = None
                for half in range(2):
                    for f in range(32):
                        mt = mm(zps[:, half * 512:(half + 1) * 512],
                                BIG[:, f * 1024 + tg * 128: f * 1024 + (tg + 1) * 128],
                                W2AR[:, f * 1024 + half * 512: f * 1024 + (half + 1) * 512],
                                f == 0, f == 31, (0, 0),
                                waits=([silu_toks[-1], last_ev[par], w2tok[-1]]
                                       if (f == 0 and half == 0) else ()),
                                inc=(1 if (f == 31 and half == 1) else None))
                if first:
                    xf2 = XF0 if par == 0 else XF1
                else:
                    xf2 = XB16 if par == 0 else XB17
                t_x = dma(xf2[:, :], src[tb * 128:(tb + 1) * 128, :], waits=[last_ev[par]],
                          ch="s0" if par == 0 else "s1")
                xin_blk = XIN[:, tg * 1024:(tg + 1) * 1024]
                t_a = vop(lambda e, x=xf2, xi=xin_blk: e.scalar_tensor_tensor(
                    FEV[:, :], xi, 0.5, x[:, :], ALU.mult, ALU.add), waits=[t_x])
                t_a = vop(lambda e: e.tensor_tensor(FEV[:, :], FEV[:, :], CB[:, 0:1024], ALU.add))
                if first:
                    stg = XB16 if par == 0 else SQS
                    t_o = vop(lambda e, z=zps, o=stg: e.scalar_tensor_tensor(
                        o[:, :], z[:, :], 0.25, FEV[:, :], ALU.mult, ALU.add),
                        waits=[mt, p.token("xo")])
                    t_w = dma(x1stash[tb * 128:(tb + 1) * 128, :], stg[:, :], waits=[t_o],
                              ch="xo", eng="gpsimd")
                else:
                    t_o = vop(lambda e, z=zps, o=(XF0 if par == 0 else XF1):
                              e.scalar_tensor_tensor(
                        o[:, :], z[:, :], 0.25, FEV[:, :],
                        ALU.mult, ALU.add), waits=[mt])
                    t_w = dma(out[tb * 128:(tb + 1) * 128, :],
                              (XF0 if par == 0 else XF1)[:, :], waits=[t_o], ch="xo",
                              eng="gpsimd")
                last_ev[par] = t_o


    XKQT = BIG[:, 0:16384]
    LTGT = BIG[:, 16384:24576]
    XQWT = BIG[:, 24576:32768]

    def proj_phase(tidx):
        t0 = tidx * TTILE
        blk_toks = []
        for b in range(8):
            tb = tidx * 8 + b
            par = b % 2
            c0 = 8 * par
            xin_blk = XIN[:, b * 1024:(b + 1) * 1024]
            tld = dma(xin_blk, x1stash[tb * 128:(tb + 1) * 128, :],
                      waits=[p.token("vector"), p.token("xo")],
                      ch="s0" if par == 0 else "s1")
            t_st = ln_stats(xin_blk, 1024, c0, 1e-5, waits=[tld])
            t_h = vop(lambda e, i=xin_blk, cc=c0: e.tensor_scalar(
                TMPB[:, :], i, STAT[:, cc:cc + 1], STAT[:, cc + 1:cc + 2],
                ALU.add, ALU.mult), waits=[t_st, p.token("tensor")])
            ttr = []
            for k in range(8):
                ttr.append(tr(PS_E[:, k * 128:(k + 1) * 128], TMPB[:, k * 128:(k + 1) * 128],
                              IDE[:, :], (0, 0), waits=[t_h] if k == 0 else (),
                              inc=(1 if k == 7 else None)))
            t_ev = vop(lambda e, bb=b: e.tensor_copy(
                HT[:, :].rearrange("q (c n) -> q c n", c=8)[:, :, bb * 128:(bb + 1) * 128],
                PS_E[:, :].rearrange("q (c n) -> q c n", c=8)), waits=[ttr[-1]])
            blk_toks.append(t_ev)
        ht_tok = blk_toks[-1]

        mt = None
        for half in range(2):
            psg = PS_C if half == 0 else PS_F
            for c in range(8):
                mt = mm(psg[0:16, :], LRW[:, c * 16:(c + 1) * 16],
                        HT[:, c * 1024 + half * 512: c * 1024 + (half + 1) * 512],
                        c == 0, c == 7, (0, 0),
                        waits=[ht_tok] if c == 0 else (), inc=(1 if c == 7 else None))
        t_tanh0 = act(SRAW[:, 0:512], PS_C[0:16, :], AF.Tanh, bias=LRB2[:, 0:1], scale=0.5,
                      waits=[mt])
        t_tanh1 = act(SRAW[:, 512:1024], PS_F[0:16, :], AF.Tanh, bias=LRB2[:, 0:1], scale=0.5,
                      waits=[mt])
        t_sig = vop(lambda e: e.tensor_scalar(SRAW[:, :], SRAW[:, :], 0.5 / 64, 0.5 / 64,
                                              ALU.mult, ALU.add),
                    waits=[t_tanh0, t_tanh1])
        ttr2 = []
        for n in range(NSTEP):
            ttr2.append(tr(PS_D[0:16, n * 16:(n + 1) * 16], SRAW[0:16, n * 16:(n + 1) * 16],
                           IDE[0:16, 0:16], (0, 0), waits=[t_sig] if n == 0 else (),
                           inc=(1 if n == NSTEP - 1 else None)))
        t_blk = vop(lambda e: e.tensor_copy(SBLK[:, :], PS_D[0:16, :]), waits=[ttr2[-1]])
        mt2 = []
        for j in range(4):
            for g in range(4):
                rhs = SBLK[0:16, (4 * g + j)::16]
                mt2.append(mm(PS_F[32 * j:32 * j + 32, 64 * g:64 * g + 64], EYE[:, :], rhs,
                              True, True, (0, 32 * j),
                              waits=[t_blk, p.token("vector"), p.token("scalar")]
                              if (j == 0 and g == 0) else (),
                              inc=(1 if (j == 3 and g == 3) else None)))
        t_sigs = vop(lambda e: e.tensor_copy(SIGS[:, :], PS_F[:, 0:256]), waits=[mt2[-1]])
        vop(lambda e: e.tensor_scalar(LASTC[:, :], SIGS[:, :], T15[:, 0:1], None, ALU.mult))

        last_pj = p.token("pj0")
        last_evac = p.token("vector")
        last_act = p.token("scalar")
        for wi, kind in ((1, "k"), (0, "q"), (2, "v"), (3, "g")):
            wtoks = []
            for c in range(8):
                wtoks.append(dma(PJW[:, c * 1024:(c + 1) * 1024], projstash[wi, c],
                                 waits=[p.token("tensor")], ch="pj0"))
            wtok = wtoks[-1]
            if kind == "g":
                for b in range(8):
                    zp = PS_A if b % 2 == 0 else PS_B
                    mt = None
                    for half in range(2):
                        for c in range(8):
                            mt = mm(zp[:, half * 512:(half + 1) * 512],
                                    HT[:, c * 1024 + b * 128: c * 1024 + (b + 1) * 128],
                                    PJW[:, c * 1024 + half * 512: c * 1024 + (half + 1) * 512],
                                    c == 0, c == 7, (0, 0),
                                    waits=[wtok, last_act, last_evac]
                                    if (c == 0 and half == 0) else (),
                                    inc=(1 if (c == 7 and half == 1) else None))
                    last_act = act(GATE[:, b * 1024:(b + 1) * 1024], zp[:, :],
                                   AF.Gelu_apprx_tanh, waits=[mt])
            else:
                for th in range(2):
                    for pig in range(2):
                        mts = []
                        for c in range(8):
                            for j in range(4):
                                ps_ = (PS_A if j < 2 else PS_B)
                                psl = ps_[:, (j % 2) * 512:(j % 2 + 1) * 512]
                                for sr in range(2):
                                    ibase = ((8 * pig + 4 * sr + j) * 64)
                                    lhs = PJW[:, c * 1024 + ibase: c * 1024 + ibase + 64]
                                    mts.append(mm(psl[64 * sr:64 * sr + 64, :], lhs,
                                            HT[:, c * 1024 + th * 512: c * 1024 + (th + 1) * 512],
                                            c == 0, c == 7, (0, 64 * sr),
                                            waits=[wtok, last_evac]
                                            if (c == 0 and j == 0 and sr == 0) else (),
                                            inc=(1 if (c == 7 and sr == 1) else None)))
                        mtok = mts[-1]
                        for j in range(4):
                            ps_ = (PS_A if j < 2 else PS_B)
                            psl = ps_[:, (j % 2) * 512:(j % 2 + 1) * 512]
                            if kind in ("q", "k"):
                                kq = 0 if kind == "k" else 1
                                dst = XKQT[:, pig * 8192 + j * 2048 + th * 1024:
                                           pig * 8192 + j * 2048 + th * 1024 + 1024]
                                dst = dst.rearrange("p (n g) -> p n g", g=32)[:, :, kq * 16:kq * 16 + 16]
                                last_evac = vop(lambda e, d=dst, sp=psl: e.tensor_copy(
                                    d, sp.rearrange("p (n t) -> p n t", t=16)), waits=[mtok])
                            else:
                                src_k = XKQT[:, pig * 8192 + j * 2048 + th * 1024:
                                             pig * 8192 + j * 2048 + th * 1024 + 1024]
                                src_k = src_k.rearrange("p (n g) -> p n g", g=32)[:, :, 0:16]
                                dst = LTGT[:, pig * 4096 + j * 1024 + th * 512:
                                           pig * 4096 + j * 1024 + th * 512 + 512]
                                last_evac = vop(lambda e, d=dst, sp=psl, sk=src_k:
                                    e.scalar_tensor_tensor(
                                        d.rearrange("p (n t) -> p n t", t=16),
                                        sp.rearrange("p (n t) -> p n t", t=16), 1.0, sk,
                                        ALU.mult, ALU.subtract), waits=[mtok])
        lt_toks = []
        for pig in range(2):
            for j in range(4):
                sl = LTGT[:, pig * 4096 + j * 1024: pig * 4096 + (j + 1) * 1024]
                lt_toks.append(vop(lambda e, ss=sl, c=pig * 4 + j: e.tensor_scalar(
                    ss, ss, LNWT8[:, c:c + 1], None, ALU.mult), waits=[last_evac]))
        return lt_toks[-1]

    def post_phase(tidx, st):
        wtoks = []
        for c in range(8):
            wtoks.append(dma(PJW[:, c * 1024:(c + 1) * 1024], projstash[4, c],
                             waits=[p.token("tensor")], ch="pj0"))
        wtok = wtoks[-1]
        tx2w = None
        tdma_prev = None
        last_gt = p.token("vector")
        for b in range(8):
            first = True
            for pi in range(2):
                for j in range(4):
                    src = XQWT[:, pi * 4096 + j * 1024 + b * 128:
                               pi * 4096 + j * 1024 + b * 128 + 128]
                    tr(PS_E[:, (pi * 4 + j) * 128:(pi * 4 + j) * 128 + 128],
                       src, IDE[:, :], (0, 0),
                       waits=[st["psE_free"], st["psB_free"], last_gt]
                       if first else (), inc=None)
                    first = False
            tre = p.wave_end()
            tev = None
            for pi in range(2):
                for j in range(4):
                    dst = TMPB[:, :].rearrange(
                        "p (pp ss jj f) -> p pp ss jj f", pp=2, ss=2, jj=4)[:, pi, :, j, :]
                    src = PS_E[:, (pi * 4 + j) * 128:(pi * 4 + j) * 128 + 128].rearrange(
                        "p (ss f) -> p ss f", ss=2)
                    tev = vop(lambda e, d=dst, s2=src: e.tensor_copy(d, s2),
                              waits=[tre] if pi == 0 and j == 0 else ())
            tst = ln_stats(TMPB[:, :], 1024, 0, 1e-6, waits=[tev])
            txh = vop(lambda e: e.tensor_scalar(
                TMPB[:, :], TMPB[:, :], STAT[:, 0:1], STAT[:, 1:2],
                ALU.add, ALU.mult), waits=[tst])
            tg1 = vop(lambda e: e.tensor_tensor(TMPB[:, :], TMPB[:, :],
                                                CB[:, 3072:4096], ALU.mult))
            tg2 = vop(lambda e: e.tensor_tensor(TMPB[:, :], TMPB[:, :],
                                                CB[:, 4096:5120], ALU.add))
            tg3 = vop(lambda e, bb=b: e.tensor_tensor(
                TMPB[:, :], TMPB[:, :], GATE[:, bb * 1024:(bb + 1) * 1024], ALU.mult))
            first = True
            for k in range(8):
                tr(PS_D[:, k * 128:(k + 1) * 128], TMPB[:, k * 128:(k + 1) * 128],
                   IDE[:, :], (0, 0), waits=[tg3, st["psD_free"]] if first else (),
                   inc=None)
                first = False
            trb = p.wave_end()
            tgt_ = vop(lambda e, bb=b: e.tensor_copy(
                GATE[:, bb * 1024:(bb + 1) * 1024], PS_D[:, :]), waits=[trb])
            last_gt = tgt_
            first = True
            for k in range(8):
                for half in range(2):
                    mm(PSPOOL[:, half * 512:(half + 1) * 512],
                       GATE[:, b * 1024 + k * 128:b * 1024 + k * 128 + 128],
                       PJW[:, k * 1024 + half * 512:k * 1024 + half * 512 + 512],
                       k == 0, k == 7, (0, 0),
                       waits=[wtok, tgt_, st["psA_free"], tx2w or st["psA_free"]]
                       if first else (), inc=None)
                    first = False
            tmo = p.wave_end()
            tx2 = None
            for half in range(2):
                tx2 = vop(lambda e, bb=b, hh=half: e.tensor_tensor(
                    TMPA[:, hh * 512:(hh + 1) * 512],
                    PSPOOL[:, hh * 512:(hh + 1) * 512],
                    XIN[:, bb * 1024 + hh * 512:bb * 1024 + hh * 512 + 512],
                    ALU.add),
                    waits=([tmo] + ([tdma_prev] if tdma_prev else []))
                    if half == 0 else ())
            tx2w = tx2
            tdma_prev = dma(x2stash[(tidx * 8 + b) * 128:(tidx * 8 + b + 1) * 128, :],
                            TMPA[:, :], waits=[tx2], ch="sto", eng="gpsimd")

    def prep2():
        g0 = [p.token("tensor"), p.token("vector"), p.token("scalar")]
        vop(lambda e: e.memset(R1T[:, :], 0.0), waits=g0)
        vop(lambda e: e.memset(R2T[:, :], 0.0))
        xf_guard = [p.token("vector"), p.token("vector")]
        for c in range(8):
            par = c % 2
            buf = XF0 if par == 0 else XF1
            tld = dma(buf[:, 0:16], lrwT[c * 128:(c + 1) * 128, :],
                      waits=[xf_guard[par]] + g0, ch="s0" if par == 0 else "s1")
            tcv = vop(lambda e, b=buf, cc=c: e.tensor_copy(LRW[:, cc * 16:(cc + 1) * 16],
                                                           b[:, 0:16]), waits=[tld])
            xf_guard[par] = tcv
        for (dst, src_) in ((W1S, w1s0), (W2S, w2s0), (W2ST, w2st0), (B1S, b1s0)):
            for half in range(2):
                par = half % 2
                buf = XF0 if par == 0 else XF1
                tld = dma(buf[:, :], src_[:, half * 1024:(half + 1) * 1024],
                          waits=[xf_guard[par]], ch="s0" if par == 0 else "s1")
                tcv = vop(lambda e, d=dst, b=buf, h=half: e.tensor_copy(
                    d[:, h * 1024:(h + 1) * 1024], b[:, :]), waits=[tld])
                xf_guard[par] = tcv
        tld = dma(XF0[:, 0:256], b2s0, waits=[xf_guard[0]], ch="s0")
        xf_guard[0] = vop(lambda e: e.tensor_copy(B2S[:, :], XF0[:, 0:256]), waits=[tld])
        return xf_guard


    CGA = GELU_A * 0.044715

    def ln_stats4(zsrc, cbase, eps, waits=()):
        s1r = STAT[:, cbase + 8:cbase + 12]
        s2r = STAT[:, cbase + 12:cbase + 16]
        negmu = STAT[:, cbase:cbase + 4]
        rstd = STAT[:, cbase + 4:cbase + 8]
        scr2 = STAT[:, cbase + 16:cbase + 20]
        t1 = vop(lambda e: e.reduce_sum(s1r, zsrc.rearrange("p (g d) -> p g d", g=4),
                                        axis=AX.X), waits=waits)
        if LCUT <= 1: return t1
        t2 = act(SQS[:, 0:256], zsrc, AF.Square, waits=waits)
        if LCUT <= 2: return t2
        t3 = vop(lambda e: e.reduce_sum(
            s2r, SQS[:, 0:256].rearrange("p (g d) -> p g d", g=4), axis=AX.X), waits=[t2])
        if LCUT <= 3: return t3
        tok = vop(lambda e: e.tensor_scalar(negmu, s1r, -1.0 / 64, None, ALU.mult),
                  waits=[t1])
        tok = vop(lambda e: e.tensor_scalar(s2r, s2r, 1.0 / 64, None, ALU.mult), waits=[t3])
        tok = vop(lambda e: e.tensor_tensor(scr2, negmu, negmu, ALU.mult))
        tok = vop(lambda e: e.tensor_tensor(s2r, s2r, scr2, ALU.subtract))
        tok = vop(lambda e: e.tensor_scalar(s2r, s2r, 0.0, 1e30, ALU.max, ALU.min))
        tok = vop(lambda e, ep=eps: e.tensor_scalar(s2r, s2r, ep, None, ALU.add))
        tok = rsqrt_ops(rstd, s2r, scr2, 4)
        return tok

    Z1ALL = PSPOOL[:, 0:2048].rearrange("p (g x) -> p g x", g=4)[:, :, 0:256]

    def emit_step(tidx, n, st):
        first = True
        for g in range(4):
            pi, sr = g // 2, g % 2
            for j in range(4):
                w0 = [st["psA_free"], st["w1s_ready"], st["proj_ready"],
                      st["b1s_ready"]] if first else ()
                first = False
                b0 = pi * 8192 + j * 2048 + n * 32
                mm(Z1PS(g)[32 * j:32 * j + 32, :],
                   XKQT[64 * sr:64 * sr + 64, b0:b0 + 32],
                   W1S[64 * sr:64 * sr + 64, (pi * 4 + j) * 256:(pi * 4 + j) * 256 + 256],
                   True, False, (64 * sr, 32 * j), waits=w0, inc=None)
                mm(Z1PS(g)[32 * j:32 * j + 32, :],
                   ONESP[64 * sr:64 * sr + 1, :],
                   B1S[64 * sr:64 * sr + 1, (pi * 4 + j) * 256:(pi * 4 + j) * 256 + 256],
                   False, False, (64 * sr, 32 * j), inc=None)
        tz1 = p.wave_end()
        if STEP_CUT <= 1: return None
        first = True
        for g in range(4):
            pi, sr = g // 2, g % 2
            for j in range(4):
                b0 = pi * 8192 + j * 2048 + n * 32
                mm(A1PS(g)[32 * j:32 * j + 16, :],
                   XKQT[64 * sr:64 * sr + 64, b0:b0 + 16],
                   XKQT[64 * sr:64 * sr + 64, b0 + 16:b0 + 32],
                   True, True, (64 * sr, 32 * j),
                   waits=[st["psC_a_free"]] if first else (), inc=None)
                first = False
        ta1 = p.wave_end()
        if STEP_CUT <= 2: return None
        first = True
        for pi in range(2):
            for j in range(4):
                src_t = LTGT[:, pi * 4096 + j * 1024 + n * 16: pi * 4096 + j * 1024 + n * 16 + 16]
                tr(PS_D[32 * j:32 * j + 16, pi * 128:pi * 128 + 128], src_t,
                   IDE[:, :], (0, 32 * j),
                   waits=[st["psD_free"], st["proj_ready"]] if first else (), inc=None)
                first = False
                src_k = XKQT[:, pi * 8192 + j * 2048 + n * 32: pi * 8192 + j * 2048 + n * 32 + 16]
                tr(PS_D[32 * j:32 * j + 16, 256 + pi * 128:256 + pi * 128 + 128],
                   src_k, IDE[:, :], (0, 32 * j), inc=None)
        ttgt = p.wave_end()
        txk = vop(lambda e: e.tensor_copy(XKTS[:, 0:256], PS_D[:, 256:512]), waits=[ttgt])
        if STEP_CUT <= 3: return None
        txb = act(XBT[:, :].rearrange("p (g x) -> p g x", g=4), Z1ALL, AF.Copy, waits=[tz1])
        tx2t = vop(lambda e: e.tensor_tensor(SQS[:, :], XBT[:, :], XBT[:, :], ALU.mult),
                   waits=[txb])
        tm = vop(lambda e: e.tensor_scalar(UB[:, :], SQS[:, :], CGA, GELU_A,
                                           ALU.mult, ALU.add))
        tu = vop(lambda e: e.tensor_tensor(UB[:, :], XBT[:, :], UB[:, :], ALU.mult))
        tt_ = act(TB[:, :], UB[:, :], AF.Tanh, waits=[tu])
        ts1 = vop(lambda e: e.tensor_scalar(S1B[:, :], TB[:, :], 0.5, 0.5,
                                            ALU.mult, ALU.add), waits=[tt_])
        tx2 = vop(lambda e: e.tensor_tensor(X2SB[:, :], XBT[:, :], S1B[:, :], ALU.mult),
                  waits=[st["x2sb_free"]])
        tq4a = vop(lambda e: e.tensor_scalar(UB[:, :], S1B[:, :], -1.0, 1.0,
                                             ALU.mult, ALU.add))
        tq4 = vop(lambda e: e.tensor_tensor(UB[:, :], S1B[:, :], UB[:, :], ALU.mult))
        tw_ = vop(lambda e: e.tensor_scalar(SQS[:, :], SQS[:, :], 2 * GELU_C3, 2 * GELU_A,
                                            ALU.mult, ALU.add))
        tr_ = vop(lambda e: e.tensor_tensor(SQS[:, :], UB[:, :], SQS[:, :], ALU.mult))
        trr = vop(lambda e: e.tensor_tensor(SQS[:, :], XBT[:, :], SQS[:, :], ALU.mult))
        tbwd = vop(lambda e: e.tensor_tensor(BWD[:, :], SQS[:, :], S1B[:, :], ALU.add))
        if STEP_CUT <= 4: return None
        first = True
        for g in range(4):
            for q in range(2):
                tr(PS_E[:, (2 * g + q) * 128:(2 * g + q) * 128 + 128],
                   X2SB[:, 256 * g + 128 * q:256 * g + 128 * q + 128],
                   IDE[:, :], (0, 0),
                   waits=[tx2, st["psE_free"]] if first else (), inc=None)
                first = False
        te = p.wave_end()
        tx2ts = vop(lambda e: e.tensor_copy(X2TS[:, :], PS_E), waits=[te])
        if STEP_CUT <= 5: return None
        first = True
        for g in range(4):
            for j in range(4):
                h = 4 * g + j
                for q in range(2):
                    if (4 * g + j) * 2 + q >= MM_LIMIT:
                        continue
                    mm(Z2PS[32 * j:32 * j + 16, 64 * g:64 * g + 64],
                       X2TS[:, (2 * g + q) * 128 + 32 * j:(2 * g + q) * 128 + 32 * j + 16],
                       W2S[:, (q * 16 + h) * 64:(q * 16 + h) * 64 + 64],
                       q == 0, q == 1, (0, 32 * j),
                       waits=[tx2ts, st["w2s_ready"], st["psC_z_free"]] if first else (),
                       inc=None)
                    first = False
        tf = p.wave_end()
        if STEP_CUT <= 6: return None
        tzb2 = vop(lambda e: e.tensor_tensor(ZB2[:, :], Z2PS[:, :], B2S[:, :],
                                             ALU.add), waits=[tf, st["b2s_ready"]])
        tst4 = ln_stats4(ZB2[:, :], 0, 1e-6, waits=[tzb2])
        if GCUT <= 1: return None
        txh = None
        for g in range(4):
            txh = vop(lambda e, gg=g: e.tensor_scalar(
                XHB[:, 64 * gg:64 * gg + 64], ZB2[:, 64 * gg:64 * gg + 64],
                STAT[:, gg:gg + 1], STAT[:, 4 + gg:5 + gg], ALU.add, ALU.mult),
                waits=[tst4])
        if GCUT <= 2: return None
        tq = vop(lambda e: e.tensor_tensor(GXB[:, :], XHB[:, :], AS_[:, :], ALU.mult))
        tq2 = vop(lambda e: e.tensor_tensor(GXB[:, :], GXB[:, :], LNWB[:, :], ALU.add))
        tgx = vop(lambda e: e.scalar_tensor_tensor(
            GX2[:, 0:256], PS_D[:, 0:256], -1.0, GXB[:, :], ALU.mult, ALU.add),
            waits=[tq2, ttgt])
        if GCUT <= 3: return None
        tm1 = vop(lambda e: e.reduce_sum(
            STAT[:, 24:28], GX2[:, 0:256].rearrange("p (g d) -> p g d", g=4), axis=AX.X))
        tpp = vop(lambda e: e.tensor_tensor(SQS[:, 0:256], GX2[:, 0:256], XHB[:, :],
                                            ALU.mult))
        tm2 = vop(lambda e: e.reduce_sum(
            STAT[:, 28:32], SQS[:, 0:256].rearrange("p (g d) -> p g d", g=4), axis=AX.X))
        tmm = vop(lambda e: e.tensor_scalar(STAT[:, 24:32], STAT[:, 24:32], -1.0 / 64,
                                            None, ALU.mult))
        if GCUT <= 4: return None
        tdd = None
        for g in range(4):
            tdd = vop(lambda e, gg=g: e.tensor_scalar(
                SQS[:, 64 * gg:64 * gg + 64], XHB[:, 64 * gg:64 * gg + 64],
                STAT[:, 28 + gg:29 + gg], None, ALU.mult), waits=[tmm])
        td1 = vop(lambda e: e.tensor_tensor(GX2[:, 0:256], GX2[:, 0:256], SQS[:, 0:256],
                                            ALU.add))
        tgz2 = None
        for g in range(4):
            tgz2 = vop(lambda e, gg=g: e.tensor_scalar(
                GZ2[:, 64 * gg:64 * gg + 64], GX2[:, 64 * gg:64 * gg + 64],
                STAT[:, 24 + gg:25 + gg], STAT[:, 4 + gg:5 + gg], ALU.add, ALU.mult),
                waits=[td1])
        tlg2 = None
        for g in range(4):
            tlg2 = vop(lambda e, gg=g: e.tensor_scalar(
                LGZ2[:, 64 * gg:64 * gg + 64], GZ2[:, 64 * gg:64 * gg + 64],
                LASTC[:, 64 * gg + n:64 * gg + n + 1], None, ALU.mult), waits=[tgz2])
        if STEP_CUT <= 7: return None
        first = True
        for pi in range(2):
            tr(PS_D[:, 512 + pi * 128:512 + pi * 128 + 128],
               GZ2[:, pi * 128:pi * 128 + 128], IDE[:, :], (0, 0),
               waits=[tgz2, txk] if first else (), inc=None)
            first = False
        th = p.wave_end()
        tgz2t = vop(lambda e: e.tensor_copy(GZ2T[:, :], PS_D[:, 512:768]), waits=[th])
        if STEP_CUT <= 8: return None
        first = True
        for g in range(4):
            pi, sr = g // 2, g % 2
            for j in range(4):
                mm(GZ1PS(g)[32 * j:32 * j + 16, :],
                   GZ2T[64 * sr:64 * sr + 64, pi * 128 + 32 * j:pi * 128 + 32 * j + 16],
                   W2ST[64 * sr:64 * sr + 64, (pi * 4 + j) * 256:(pi * 4 + j) * 256 + 256],
                   True, True, (64 * sr, 32 * j),
                   waits=[tgz2t, st["w2st_ready"], st["psB_free"]] if first else (),
                   inc=None)
                first = False
        ti = p.wave_end()
        if STEP_CUT <= 9: return None
        tgz1 = None
        for g in range(4):
            tgz1 = vop(lambda e, gg=g: e.tensor_tensor(
                GZ1[:, 256 * gg:256 * gg + 256], GZ1PS(gg),
                BWD[:, 256 * gg:256 * gg + 256], ALU.mult),
                waits=[ti, tbwd] if g == 0 else ())
        tlg1 = None
        for g in range(4):
            tlg1 = vop(lambda e, gg=g: e.tensor_scalar(
                LGZ1[:, 256 * gg:256 * gg + 256], GZ1[:, 256 * gg:256 * gg + 256],
                LASTC[:, 64 * gg + n:64 * gg + n + 1], None, ALU.mult), waits=[tgz1])
        if STEP_CUT <= 10: return None
        tr1 = None
        for g in range(4):
            t1_ = vop(lambda e, gg=g: e.tensor_scalar(
                GX2[:, 0:16], A1PS(gg)[:, :],
                1.0, SIGS[:, 64 * gg + n:64 * gg + n + 1], ALU.add, ALU.mult),
                waits=[ta1, tgx, ti] if g == 0 else ())
            tr1 = vop(lambda e, gg=g: e.tensor_tensor(
                R1T[:, 32 * gg + 16:32 * gg + 32], GX2[:, 0:16], NTTE[:, :], ALU.mult))
        first = True
        for g in range(4):
            for j in range(4):
                mm(Z1PS(g)[32 * j:32 * j + 32, :],
                   R1T[32 * j:32 * j + 32, 32 * g:32 * g + 32],
                   GZ1[32 * j:32 * j + 32, 256 * g:256 * g + 256],
                   False, True, (32 * j, 32 * j),
                   waits=[tr1, tlg1, txb] if first else (), inc=None)
                first = False
        tc1 = p.wave_end()
        if STEP_CUT <= 11: return None
        tx2b = act(X2SB[:, :].rearrange("p (g x) -> p g x", g=4), Z1ALL,
                   AF.Gelu_apprx_tanh, waits=[tc1, tx2ts])
        if STEP_CUT <= 12: return None
        first = True
        for g in range(4):
            for q in range(2):
                tr(PS_E[:, (2 * g + q) * 128:(2 * g + q) * 128 + 128],
                   X2SB[:, 256 * g + 128 * q:256 * g + 128 * q + 128],
                   IDE[:, :], (0, 0), waits=[tx2b, tf] if first else (), inc=None)
                first = False
        tml = p.wave_end()
        tx2ts2 = vop(lambda e: e.tensor_copy(X2TS[:, :], PS_E), waits=[tml])
        if STEP_CUT <= 13: return None
        first = True
        for g in range(4):
            for j in range(4):
                h = 4 * g + j
                for q in range(2):
                    mm(Z2PS[32 * j:32 * j + 32, 64 * g:64 * g + 64],
                       X2TS[:, (2 * g + q) * 128 + 32 * j:(2 * g + q) * 128 + 32 * j + 32],
                       W2S[:, (q * 16 + h) * 64:(q * 16 + h) * 64 + 64],
                       q == 0, q == 1, (0, 32 * j),
                       waits=[tx2ts2, tzb2] if first else (), inc=None)
                    first = False
        tnp = p.wave_end()
        first = True
        for g in range(4):
            for j in range(4):
                for q in range(2):
                    mm(A2PS(g)[32 * j:32 * j + 16, :],
                       X2TS[:, (2 * g + q) * 128 + 32 * j:(2 * g + q) * 128 + 32 * j + 16],
                       X2TS[:, (2 * g + q) * 128 + 32 * j + 16:
                            (2 * g + q) * 128 + 32 * j + 32],
                       q == 0, q == 1, (0, 32 * j),
                       waits=[st["psC_a2_free"]] if first else (),
                       inc=None)
                    first = False
        to_ = p.wave_end()
        if STEP_CUT <= 14: return None
        tr2 = None
        for g in range(4):
            t1_ = vop(lambda e, gg=g: e.tensor_scalar(
                GX2[:, 0:16], A2PS(gg)[:, :],
                1.0, SIGS[:, 64 * gg + n:64 * gg + n + 1], ALU.add, ALU.mult),
                waits=[to_])
            tr2 = vop(lambda e, gg=g: e.tensor_tensor(
                R2T[:, 32 * gg + 16:32 * gg + 32], GX2[:, 0:16], NTTE[:, :], ALU.mult))
        first = True
        for g in range(4):
            for j in range(4):
                mm(PSPOOL[32 * j:32 * j + 32, 1280 + 64 * g:1280 + 64 * g + 64],
                   R2T[32 * j:32 * j + 32, 32 * g:32 * g + 32],
                   GZ2[32 * j:32 * j + 32, 64 * g:64 * g + 64],
                   True, False, (32 * j, 32 * j),
                   waits=[tr2, st["b2s_ready"]] if first else (), inc=None)
                first = False
                mm(PSPOOL[32 * j:32 * j + 32, 1280 + 64 * g:1280 + 64 * g + 64],
                   ONESP[32 * j:32 * j + 1, :],
                   B2S[32 * j:32 * j + 1, 64 * g:64 * g + 64],
                   False, True, (32 * j, 32 * j), inc=None)
        tpq = p.wave_end()
        if STEP_CUT <= 15: return None
        tzc = act(ZB2[:, :], Z2PS[:, :], AF.Copy, waits=[tnp, tr2, tgz2])
        tzb2b = vop(lambda e: e.tensor_tensor(ZB2[:, :], ZB2[:, :],
                                              PSPOOL[:, 1280:1536], ALU.add),
                    waits=[tzc, tpq])
        tst5 = ln_stats4(ZB2[:, :], 32, 1e-6, waits=[tzb2b])
        txh2 = None
        for g in range(4):
            txh2 = vop(lambda e, gg=g: e.tensor_scalar(
                XHB[:, 64 * gg:64 * gg + 64], ZB2[:, 64 * gg:64 * gg + 64],
                STAT[:, 32 + gg:33 + gg], STAT[:, 36 + gg:37 + gg], ALU.add, ALU.mult),
                waits=[tst5])
        tn1 = vop(lambda e: e.tensor_tensor(GXB[:, :], XHB[:, :], LNWS[:, :], ALU.mult),
                  waits=[txh2])
        txqw = vop(lambda e: e.tensor_tensor(XQWS[:, :], GXB[:, :], LNBS[:, :], ALU.add))
        if STEP_CUT <= 16: return None
        first = True
        for pi in range(2):
            tr(PS_D[:, 256 + pi * 128:256 + pi * 128 + 128],
               XQWS[:, pi * 128:pi * 128 + 128], IDE[:, :], (0, 0),
               waits=[txqw, tgz2t] if first else (), inc=None)
            first = False
        tsx = p.wave_end()
        tev = None
        for pi in range(2):
            src1 = PS_D[:, 256 + pi * 128:256 + pi * 128 + 128].rearrange(
                "p (j t) -> p j t", t=32)[:, :, 16:32]
            dst = XQWT[:, 0:8192].rearrange("p (pp2 j x) -> p pp2 j x", pp2=2, j=4)[
                :, pi, :, n * 16:n * 16 + 16]
            xq = XKQT[:, 0:16384].rearrange("p (pp2 j x) -> p pp2 j x", pp2=2, j=4)[
                :, pi, :, n * 32 + 16:n * 32 + 32]
            tev = vop(lambda e, d=dst, s1_=src1, x_=xq: e.tensor_tensor(
                d, s1_, x_, ALU.add), waits=[tsx, st["proj_ready"]])
        if STEP_CUT <= 17: return None
        tw1sub = st.get("w1sub_prev")
        for pi in range(2):
            first = True
            for sr2 in range(2):
                g = 2 * pi + sr2
                for j in range(4):
                    mm(PSPOOL[64 * sr2:64 * sr2 + 64, 512 * j + 256:512 * j + 512],
                       XKTS[32 * j:32 * j + 16, pi * 128 + 64 * sr2:pi * 128 + 64 * sr2 + 64],
                       LGZ1[32 * j:32 * j + 16, 256 * g:256 * g + 256],
                       True, True, (32 * j, 64 * sr2),
                       waits=([txk, tlg1, tgz1] + ([tw1sub] if tw1sub else []))
                       if first else (), inc=None)
                    first = False
            td = p.wave_end()
            tw1sub = vop(lambda e, pp=pi: e.tensor_tensor(
                W1S[:, pp * 1024:pp * 1024 + 1024].rearrange("p (j x) -> p j x", j=4),
                W1S[:, pp * 1024:pp * 1024 + 1024].rearrange("p (j x) -> p j x", j=4),
                PSPOOL[:, 0:2048].rearrange("p (j x) -> p j x", j=4)[:, :, 256:512],
                ALU.subtract), waits=[td])
        st["w1sub_prev"] = tw1sub
        if TCUT <= 1: return None
        tw2tsub = None
        for pi in range(2):
            first = True
            for sr2 in range(2):
                g = 2 * pi + sr2
                for j in range(4):
                    mm(PSPOOL[64 * sr2:64 * sr2 + 64, 512 * j + 256:512 * j + 512],
                       LGZ2[32 * j:32 * j + 16, 64 * g:64 * g + 64],
                       X2SB[32 * j:32 * j + 16, 256 * g:256 * g + 256],
                       True, True, (32 * j, 64 * sr2),
                       waits=([tlg2, tw1sub, tx2b] if pi == 0 else [tw2tsub])
                       if first else (), inc=None)
                    first = False
            td = p.wave_end()
            tw2tsub = vop(lambda e, pp=pi: e.tensor_tensor(
                W2ST[:, pp * 1024:pp * 1024 + 1024].rearrange("p (j x) -> p j x", j=4),
                W2ST[:, pp * 1024:pp * 1024 + 1024].rearrange("p (j x) -> p j x", j=4),
                PSPOOL[:, 0:2048].rearrange("p (j x) -> p j x", j=4)[:, :, 256:512],
                ALU.subtract), waits=[td])
        if TCUT <= 2: return None
        tw2sub = None
        for q in range(2):
            first = True
            for g in range(4):
                for j in range(4):
                    for sg in range(2):
                        mm(PSPOOL[64 * sg:64 * sg + 64, 512 * j + 64 * g:512 * j + 64 * g + 64],
                           X2SB[32 * j:32 * j + 16,
                                256 * g + 128 * q + 64 * sg:256 * g + 128 * q + 64 * sg + 64],
                           LGZ2[32 * j:32 * j + 16, 64 * g:64 * g + 64],
                           True, True, (32 * j, 64 * sg),
                           waits=[tlg2, tx2b, tw2tsub if q == 0 else tw2sub]
                           if first else (), inc=None)
                        first = False
            td = p.wave_end()
            tw2sub = vop(lambda e, qq=q: e.tensor_tensor(
                W2S[:, qq * 1024:qq * 1024 + 1024].rearrange(
                    "p (g j d) -> p g j d", g=4, j=4),
                W2S[:, qq * 1024:qq * 1024 + 1024].rearrange(
                    "p (g j d) -> p g j d", g=4, j=4),
                PSPOOL[:, 0:2048].rearrange("p (j x) -> p j x", j=4)[
                    :, :, 0:256].rearrange("p j (g d) -> p g j d", g=4),
                ALU.subtract), waits=[td])
        if TCUT <= 3: return None
        first = True
        for g in range(4):
            sr = g % 2
            pi = g // 2
            for j in range(4):
                mm(PSPOOL[64 * sr:64 * sr + 32, 512 * j + 256 * pi:512 * j + 256 * pi + 256],
                   ONESP[32 * j:32 * j + 16, :],
                   LGZ1[32 * j:32 * j + 16, 256 * g:256 * g + 256],
                   True, True, (32 * j, 64 * sr),
                   waits=[tw2sub, tw2tsub, tlg1] if first else (), inc=None)
                first = False
        tbd = p.wave_end()
        tb1sub = vop(lambda e: e.tensor_tensor(
            B1S[:, :].rearrange("p (pi j x) -> p pi j x", pi=2, j=4),
            B1S[:, :].rearrange("p (pi j x) -> p pi j x", pi=2, j=4),
            PSPOOL[:, 0:2048].rearrange("p (j pi x) -> p pi j x", pi=2, x=256),
            ALU.subtract), waits=[tbd])
        if TCUT <= 4: return None
        first = True
        for g in range(4):
            for j in range(4):
                mm(PSPOOL[32 * j:32 * j + 32, 2048 + 64 * g:2048 + 64 * g + 64],
                   ONESP[32 * j:32 * j + 16, :],
                   LGZ2[32 * j:32 * j + 16, 64 * g:64 * g + 64],
                   True, True, (32 * j, 32 * j),
                   waits=[tzc, tlg2, tev, tx2ts2] if first else (),
                   inc=None)
                first = False
        tbd2 = p.wave_end()
        tb2sub = vop(lambda e: e.tensor_tensor(
            B2S[:, :], B2S[:, :], PSPOOL[:, 2048:2304], ALU.subtract), waits=[tbd2])
        if STEP_CUT <= 18: return None
        st["psA_free"] = tb1sub
        st["psA_free2"] = tb1sub
        st["psB_free"] = tb2sub
        st["psC_z_free"] = tb2sub
        st["psC_a_free"] = tb2sub
        st["psC_a2_free"] = tb2sub
        st["psD_free"] = tb2sub
        st["psE_free"] = tb2sub
        st["x2sb_free"] = tw2sub
        st["w1s_ready"] = tw1sub
        st["w2s_ready"] = tw2sub
        st["w2st_ready"] = tw2tsub
        st["b1s_ready"] = tb1sub
        st["b2s_ready"] = tb2sub
        return st


    prep()
    ffn_phase(True)
    if STAGE >= 2:
        prep2()
        st = None
        for tidx in range(NTILES_RUN):
            tproj = proj_phase(tidx)
            if STAGE < 3:
                break
            bar_t = vop(lambda e: e.engine_nop(),
                        waits=[p.token("tensor"), p.token("scalar"), p.token("gpsimd"),
                               tproj])
            if st is None:
                st = {k: bar_t for k in
                      ["psA_free", "psA_free2", "psB_free", "psC_z_free", "psC_a_free",
                       "psC_a2_free", "psD_free", "psE_free", "psF_free", "x2sb_free",
                       "w1s_ready", "w2s_ready", "w2st_ready", "b1s_ready", "b2s_ready",
                       "proj_ready"]}
            st["proj_ready"] = bar_t
            aborted = False
            for n_ in range(min(NSTEPS_DBG, NSTEP)):
                if emit_step(tidx, n_, st) is None:
                    aborted = True
                    break
            if aborted or STAGE < 4:
                break
            post_phase(tidx, st)
        if STAGE >= 4:
            vop(lambda e: e.engine_nop(),
                waits=[p.token("tensor"), p.token("scalar"), p.token("gpsimd")])
            ffn_phase(False)

    def finalize():
        for ch in P.CHS:
            v = p.cnt[ch]
            if v:
                p.emit("sync", lambda e, c=ch, vv=v: e.wait_ge(sems[c], vv), inc=None)

    finalize()

    with nc.Block() as block:
        def runner(eng_name):
            def _run(eng):
                for (waits, fn, inc, sem, drain) in p.q[eng_name]:
                    for (we, wv) in waits:
                        eng.wait_ge(sems[we], wv)
                    r = fn(eng)
                    if inc:
                        r.then_inc(sems[sem], inc)
                    if drain:
                        eng.drain()
            return _run
        block.tensor(runner("tensor"))
        block.vector(runner("vector"))
        block.scalar(runner("scalar"))
        block.gpsimd(runner("gpsimd"))
        block.sync(runner("sync"))
    ctx.close()
    return nc


def prep_inputs(I):
    f = {}
    f["wqT"] = np.ascontiguousarray(I["L0_attn_Wq"].T)
    f["wkT"] = np.ascontiguousarray(I["L0_attn_Wk"].T)
    f["wvT"] = np.ascontiguousarray(I["L0_attn_Wv"].T)
    f["wgT"] = np.ascontiguousarray(I["L0_attn_Wg"].T)
    f["woT"] = np.ascontiguousarray(I["L0_attn_Wo"].T)
    f["q1"] = np.ascontiguousarray(I["L0_ffn_fc1_q"])
    f["sc1"] = np.ascontiguousarray(I["L0_ffn_fc1_scales"])
    f["q2"] = np.ascontiguousarray(I["L0_ffn_fc2_q"])
    f["sc2"] = np.ascontiguousarray(I["L0_ffn_fc2_scales"])
    f["fc1b"] = np.ascontiguousarray(I["L0_ffn_fc1_bias"].reshape(32, 128).T)
    f["fc2b"] = np.ascontiguousarray(I["L0_ffn_fc2_bias"].reshape(1, 1024))
    f["lng"] = np.ascontiguousarray(I["L0_ffn_ln_gamma"].reshape(1, 1024))
    f["lnbta"] = np.ascontiguousarray(I["L0_ffn_ln_beta"].reshape(1, 1024))
    f["postw"] = np.ascontiguousarray(I["L0_attn_post_w"].reshape(1, 1024))
    f["postb"] = np.ascontiguousarray(I["L0_attn_post_b"].reshape(1, 1024))
    f["lrwT"] = np.ascontiguousarray(I["L0_attn_lr_w"][:, 0, :].T)
    f["lrb"] = np.ascontiguousarray(I["L0_attn_lr_b"].reshape(16, 1))
    f["ltk"] = np.ascontiguousarray(I["L0_attn_learnable_token_idx"].reshape(1, 16))
    f["recip16"] = (1.0 / np.arange(1, 17, dtype=np.float64)).astype(np.float32).reshape(1, 16)
    nw = np.asarray(I["L0_attn_norm_w"], np.float32)
    nb = np.asarray(I["L0_attn_norm_b"], np.float32)
    lnw_slot = np.zeros((128, 256), np.float32)
    lnb_slot = np.zeros((128, 256), np.float32)
    for g in range(4):
        for j in range(4):
            h = 4 * g + j
            lnw_slot[32 * j:32 * j + 32, 64 * g:64 * g + 64] = nw[h][None, :]
            lnb_slot[32 * j:32 * j + 32, 64 * g:64 * g + 64] = nb[h][None, :]
    f["lnw_slot"] = lnw_slot
    f["lnb_slot"] = lnb_slot
    lnwT8 = np.zeros((128, 8), np.float32)
    for pi in range(2):
        for s in range(2):
            for j in range(4):
                h = 4 * (2 * pi + s) + j
                lnwT8[64 * s:64 * s + 64, pi * 4 + j] = nw[h]
    f["lnwT8"] = lnwT8
    W1 = np.asarray(I["L0_attn_W1"], np.float32)
    W2 = np.asarray(I["L0_attn_W2"], np.float32)
    b1 = np.asarray(I["L0_attn_b1"], np.float32)
    b2 = np.asarray(I["L0_attn_b2"], np.float32)
    w1s0 = np.zeros((128, 2048), np.float32)
    w2st0 = np.zeros((128, 2048), np.float32)
    for pi in range(2):
        for s in range(2):
            for j in range(4):
                h = 4 * (2 * pi + s) + j
                cbase = (pi * 4 + j) * 256
                w1s0[64 * s:64 * s + 64, cbase:cbase + 256] = W1[h]
                w2st0[64 * s:64 * s + 64, cbase:cbase + 256] = W2[h].T
    f["w1s0"] = w1s0
    f["w2st0"] = w2st0
    w2s0 = np.zeros((128, 2048), np.float32)
    for qh in range(2):
        for h in range(16):
            w2s0[:, (qh * 16 + h) * 64:(qh * 16 + h) * 64 + 64] = W2[h, qh * 128:(qh + 1) * 128, :]
    f["w2s0"] = w2s0
    b1s0 = np.zeros((128, 2048), np.float32)
    b2s0 = np.zeros((128, 256), np.float32)
    for pi in range(2):
        for sr in range(2):
            for j in range(4):
                h = 4 * (2 * pi + sr) + j
                b1s0[64 * sr:64 * sr + 32, (pi * 4 + j) * 256:(pi * 4 + j) * 256 + 256] = b1[h, 0][None, :]
    for g in range(4):
        for j in range(4):
            h = 4 * g + j
            b2s0[32 * j:32 * j + 32, 64 * g:64 * g + 64] = b2[h, 0][None, :]
    f["b1s0"] = b1s0
    f["b2s0"] = b2s0
    f["identc"] = np.eye(128).astype(BF)
    ntril = np.zeros((128, 16), np.float32)
    for j in range(4):
        for tp_ in range(16):
            for t in range(16):
                if tp_ <= t:
                    ntril[32 * j + tp_, t] = -1.0
    f["ntrilT"] = ntril
    eye16p = np.zeros((16, 32), BF)
    eye16p[:, 0:16] = np.eye(16)
    f["eye16p"] = eye16p
    f["onesr"] = np.ones((1, 128), BF)
    f["onesP32"] = np.ones((128, 32), BF)
    for k in f:
        a = f[k]
        if a.dtype == np.float64:
            f[k] = a.astype(np.float32)
    return f


_NC_CACHE = {}


def kernel(**inputs):
    if "nc" not in _NC_CACHE:
        _NC_CACHE["nc"] = build_program()
    nc = _NC_CACHE["nc"]
    shared = prep_inputs(inputs)
    x = np.asarray(inputs["L0_attn_hidden_states"], np.float32)
    in_maps = []
    ncores = x.shape[0]
    for c in range(ncores):
        m = dict(shared)
        m["x"] = np.ascontiguousarray(x[c])
        in_maps.append(m)
    res = run_bass_kernel_spmd(nc, in_maps, core_ids=list(range(ncores)))
    outs = np.stack([np.asarray(r["out"]) for r in res.results], axis=0)
    return outs


